# version 15
# speedup vs baseline: 1.1949x; 1.0040x over previous
import contextlib
import numpy as np

import concourse.bass as bass
import concourse.mybir as mybir
from concourse.bass_utils import run_bass_kernel_spmd

F32 = mybir.dt.float32
BF16 = mybir.dt.bfloat16
ALU = mybir.AluOpType
ACT = mybir.ActivationFunctionType
AX = mybir.AxisListType

D = 1024
H = 16
NE = 16
SL = 64
ES = 1024
FF = 2048
NCH = 8
EPS = 1e-5
STOK = 256
NEG = -3.0e38
DEBUG_SPILL = False
import os as _os
M1SUB = int(_os.environ.get('M1SUB', '99'))
WKVSUB = int(_os.environ.get('WKVSUB', '99'))


def build_core_program(T=8192, max_phase=9):
    assert T % STOK == 0
    NT = T // 128
    NST = T // STOK
    nc = bass.Bass("TRN2", target_bir_lowering=False)

    dd = {}
    for nm, sh in [
        ("L0_attn_x", [T, D]), ("L0_attn_Wk", [D, D]), ("L0_attn_Wv", [D, D]),
        ("L0_attn_Wr", [D, D]), ("L0_attn_Wo", [D, D]),
        ("L0_attn_time_decay", [H, 64]), ("L0_attn_time_first", [H, 64]),
        ("L0_ffn_norm_g", [D]), ("L0_ffn_norm_b", [D]),
        ("L0_ffn_slot_g", [D]), ("L0_ffn_slot_b", [D]), ("L0_ffn_scale", [1]),
        ("L0_ffn_slot_embeds", [NE, SL, D]),
        ("L0_ffn_Wg", [NE, D, FF]), ("L0_ffn_bg", [NE, FF]),
        ("L0_ffn_Wu", [NE, D, FF]), ("L0_ffn_bu", [NE, FF]),
        ("L0_ffn_Wd", [NE, FF, D]), ("L0_ffn_bd", [NE, D]),
        ("c_eye", [128, 128]), ("c_delta", [128, 128]), ("c_ramp1", [128, 128]),
        ("c_rrev", [128, 1]), ("c_ones", [128, 1]),
    ]:
        dd[nm] = nc.dram_tensor(nm, sh, F32, kind="ExternalInput").ap()
    d_out = nc.dram_tensor("out", [T, D], F32, kind="ExternalOutput").ap()

    s_wT = nc.dram_tensor("s_wT", [4, D, D], F32).ap()
    s_wTh = nc.dram_tensor("s_wTh", [4, D, D], BF16).ap()
    s_wTl = nc.dram_tensor("s_wTl", [4, D, D], BF16).ap()
    s_seTh = nc.dram_tensor("s_seTh", [D, ES], BF16).ap()
    s_seTl = nc.dram_tensor("s_seTl", [D, ES], BF16).ap()
    s_xf = nc.dram_tensor("s_xf", [T, D], F32).ap()
    s_lA = nc.dram_tensor("s_lA", [T, ES], F32).ap()
    s_lB = nc.dram_tensor("s_lB", [ES, T], F32).ap()
    s_p = nc.dram_tensor("s_p", [T, D], F32).ap()
    s_vec = nc.dram_tensor("s_vec", [4, ES], F32).ap()
    s_B = nc.dram_tensor("s_B", [T], F32).ap()
    if DEBUG_SPILL:
        s_dslots = nc.dram_tensor("s_dslots", [NCH * 128, ES], F32).ap()
        s_deout = nc.dram_tensor("s_deout", [128, NCH, D], F32).ap()
        s_dzm = nc.dram_tensor("s_dzm", [4, ES], F32).ap()

    st = contextlib.ExitStack()
    with st:
        blk = st.enter_context(nc.Block())
        SEM = {}
        CNT = {}
        for s in ("sIN", "sOUT", "sCN", "sGD", "sTE", "sVE", "sSE", "sGP", "sBAR"):
            SEM[s] = st.enter_context(nc.semaphore(s))
            CNT[s] = 0

        def inc(instr, s, n=1):
            instr.then_inc(SEM[s], n)
            CNT[s] += n
            return CNT[s]

        def dma(eng, *a, sem="sIN", **kw):
            eng.wait_ge(SEM[sem], CNT[sem])
            eng.dma_start(*a, **kw).then_inc(SEM[sem], 16)
            CNT[sem] += 16
            return (sem, CNT[sem])

        def wtok(eng, tok):
            eng.wait_ge(SEM[tok[0]], tok[1])

        _uniq = [0]

        def sb(stk, name, shape, dt_=F32):
            _uniq[0] += 1
            return stk.enter_context(nc.sbuf_tensor(f"{name}_{_uniq[0]}", shape, dt_))

        def psum(stk, name, shape):
            _uniq[0] += 1
            return stk.enter_context(nc.psum_tensor(f"{name}_{_uniq[0]}", shape, F32))

        def barrier():
            def _t(te): inc(te.drain(), "sBAR")
            def _v(ve): inc(ve.drain(), "sBAR")
            def _s(se): inc(se.drain(), "sBAR")
            def _g(ge):
                ge.wait_ge(SEM["sOUT"], CNT["sOUT"])
                ge.wait_ge(SEM["sGD"], CNT["sGD"])
                inc(ge.drain(), "sBAR")
            def _y(ye):
                ye.wait_ge(SEM["sIN"], CNT["sIN"])
                ye.wait_ge(SEM["sCN"], CNT["sCN"])
                inc(ye.nop(), "sBAR")
            blk.tensor(_t); blk.vector(_v); blk.scalar(_s); blk.gpsimd(_g); blk.sync(_y)
            tgt = CNT["sBAR"]
            def _wt(te): te.wait_ge(SEM["sBAR"], tgt)
            def _wv(ve): ve.wait_ge(SEM["sBAR"], tgt)
            def _ws(se): se.wait_ge(SEM["sBAR"], tgt)
            def _wg(ge): ge.wait_ge(SEM["sBAR"], tgt)
            def _wy(ye): ye.wait_ge(SEM["sBAR"], tgt)
            blk.tensor(_wt); blk.vector(_wv); blk.scalar(_ws); blk.gpsimd(_wg); blk.sync(_wy)

        state = sb(st, "state", [128, 512])
        st_isg1 = sb(st, "st_isg1", [128, NT])
        st_nb1 = sb(st, "st_nb1", [128, NT])
        st_var1 = sb(st, "st_var1", [128, NT])
        st_negmq = sb(st, "st_negmq", [128, NT])
        st_zq = sb(st, "st_zq", [128, NT])
        m_run = sb(st, "m_run", [128, 8])
        eye = sb(st, "eye", [128, 128])
        dh = sb(st, "dh", [128, H * 128])
        ftil = sb(st, "ftil", [128, NCH, 128])
        fbar = sb(st, "fbar", [128, D])
        fw = sb(st, "fw", [128, 512])
        tf_fm = sb(st, "tf_fm", [128, NCH])
        bgT = sb(st, "bgT", [128, NE, NE])
        buT = sb(st, "buT", [128, NE, NE])
        onescol = sb(st, "onescol", [128, 1])
        cnh = sb(st, "cnh", [128, 1])

        w_sum = sb(st, "w_sum", [128, 4])
        w_sq = sb(st, "w_sq", [128, 4])
        w_mu = sb(st, "w_mu", [128, 4])
        w_var = sb(st, "w_var", [128, 4])
        w_isg = sb(st, "w_isg", [128, 4])
        w_nb = sb(st, "w_nb", [128, 4])
        w_t = sb(st, "w_t", [128, 4])
        dump = sb(st, "dump", [128, D])

        def ln_stats(j, src_ap, n_free, waits):
            def _v(ve):
                for s_, v_ in waits:
                    ve.wait_ge(SEM[s_], v_)
                inc(ve.reduce_sum(w_sum[:, j:j + 1], src_ap(), axis=AX.X), "sVE")
            blk.vector(_v)
            def _s(se):
                for s_, v_ in waits:
                    se.wait_ge(SEM[s_], v_)
                se.wait_ge(SEM["sSE"], CNT["sSE"])
                inc(se.activation(out=dump[:, 0:n_free], in_=src_ap(), func=ACT.Square,
                                  accum_out=w_sq[:, j:j + 1]), "sSE")
            blk.scalar(_s)
            stgt = CNT["sSE"]
            vnow = CNT["sVE"]
            def _v2(ve):
                ve.wait_ge(SEM["sVE"], vnow)
                ve.wait_ge(SEM["sSE"], stgt)
                r = 1.0 / n_free
                inc(ve.tensor_scalar(out=w_mu[:, j:j + 1], in0=w_sum[:, j:j + 1],
                                     scalar1=r, scalar2=None, op0=ALU.mult), "sVE")
                inc(ve.tensor_scalar(out=w_var[:, j:j + 1], in0=w_sq[:, j:j + 1],
                                     scalar1=r, scalar2=None, op0=ALU.mult), "sVE")
                ve.drain()
                inc(ve.tensor_tensor(out=w_t[:, j:j + 1], in0=w_mu[:, j:j + 1],
                                     in1=w_mu[:, j:j + 1], op=ALU.mult), "sVE")
                ve.drain()
                inc(ve.tensor_tensor(out=w_var[:, j:j + 1], in0=w_var[:, j:j + 1],
                                     in1=w_t[:, j:j + 1], op=ALU.subtract), "sVE")
                ve.drain()
                inc(ve.tensor_scalar(out=w_var[:, j:j + 1], in0=w_var[:, j:j + 1],
                                     scalar1=EPS, scalar2=None, op0=ALU.add), "sVE")
            blk.vector(_v2)
            vtgt2 = CNT["sVE"]
            def _g(ge):
                ge.wait_ge(SEM["sVE"], vtgt2)
                inc(ge.tensor_tensor(out=w_isg[:, j:j + 1], in0=w_var[:, j:j + 1],
                                     in1=cnh[:], op=ALU.pow), "sGP")
            blk.gpsimd(_g)
            gtgt = CNT["sGP"]
            def _v3(ve):
                ve.wait_ge(SEM["sGP"], gtgt)
                inc(ve.tensor_scalar(out=w_nb[:, j:j + 1], in0=w_mu[:, j:j + 1],
                                     scalar1=w_isg[:, j:j + 1], scalar2=-1.0,
                                     op0=ALU.mult, op1=ALU.mult), "sVE")
            blk.vector(_v3)
            return CNT["sVE"]

        pro = contextlib.ExitStack()
        with pro:
            t_nat = sb(pro, "t_nat", [128, 2, D])
            t_slab = sb(pro, "t_slab", [128, 2, D])
            t_sh = sb(pro, "t_sh", [128, 2, D], BF16)
            t_sl = sb(pro, "t_sl", [128, 2, D], BF16)
            lnw_fm = sb(pro, "lnw_fm", [128, NCH])
            lnw_hcol = sb(pro, "lnw_hcol", [128, H])
            fwsm = sb(pro, "fwsm", [128, 8])
            lnw_rep = sb(pro, "lnw_rep", [128, D])
            delta = sb(pro, "delta", [128, 128])
            ramp1 = sb(pro, "ramp1", [128, 128])
            rrev = sb(pro, "rrev", [128, 1])
            grep = sb(pro, "grep", [128, D])
            brep = sb(pro, "brep", [128, D])
            screp = sb(pro, "screp", [128, 1])
            tmp = sb(pro, "tmp", [128, D])
            bsm = sb(pro, "bsm", [16, FF])
            ps_tr = psum(pro, "ps_tr", [128, 8, 128])
            ps_b = psum(pro, "ps_b", [128, 512])

            def _y(ye):
                dma(ye, out=eye[:], in_=dd["c_eye"], sem="sIN")
                dma(ye, out=onescol[:], in_=dd["c_ones"], sem="sIN")
                dma(ye, out=delta[:], in_=dd["c_delta"], sem="sIN")
                dma(ye, out=ramp1[:], in_=dd["c_ramp1"], sem="sIN")
                dma(ye, out=rrev[:], in_=dd["c_rrev"], sem="sIN")
                td = dd["L0_attn_time_decay"]
                tf = dd["L0_attn_time_first"]
                dma(ye, out=tmp[:, 0:NCH],
                             in_=bass.AP(td.tensor, 0, [[1, 128], [128, NCH]]), allow_slow_non_contiguous=True, sem="sIN")
                dma(ye, out=tf_fm[:],
                             in_=bass.AP(tf.tensor, 0, [[1, 128], [128, NCH]]), allow_slow_non_contiguous=True, sem="sIN")
            blk.sync(_y)

            def _g0(ge):
                inc(ge.memset(cnh[:], -0.5), "sGP")
            blk.gpsimd(_g0)

            def _s(se):
                se.wait_ge(SEM["sIN"], CNT["sIN"])
                inc(se.activation(out=lnw_fm[:], in_=tmp[:, 0:NCH], func=ACT.Ln), "sSE")
            blk.scalar(_s)
            t1 = CNT["sSE"]

            def _g(ge):
                ge.wait_ge(SEM["sSE"], t1)
                dma(ge, out=bass.AP(s_vec.tensor, 0, [[1, 128], [128, NCH]]),
                             in_=lnw_fm[:], allow_slow_non_contiguous=True, sem="sOUT")
            blk.gpsimd(_g)

            def _y2(ye):
                ye.wait_ge(SEM["sOUT"], CNT["sOUT"])
                dma(ye, out=lnw_rep[:],
                             in_=bass.AP(s_vec.tensor, 0, [[0, 128], [1, D]]), sem="sIN")
                dma(ye, out=lnw_hcol[:],
                             in_=bass.AP(s_vec.tensor, 0, [[0, 128], [64, H]]), allow_slow_non_contiguous=True, sem="sIN")
                dma(ye, out=fwsm[:],
                             in_=bass.AP(s_vec.tensor, 0, [[1, 128], [128, 8]]),
                             allow_slow_non_contiguous=True, sem="sIN")
            blk.sync(_y2)
            t2 = CNT["sIN"]

            def _s2(se):
                se.wait_ge(SEM["sIN"], t2)
                for h in range(H):
                    inc(se.activation(out=dh[:, h * 128:(h + 1) * 128], in_=delta[:],
                                      func=ACT.Exp, scale=lnw_hcol[:, h:h + 1]), "sSE")
                for c in range(NCH):
                    inc(se.activation(out=ftil[:, c, :], in_=ramp1[:],
                                      func=ACT.Exp, scale=lnw_fm[:, c:c + 1]), "sSE")
            blk.scalar(_s2)

            def _v(ve):
                ve.wait_ge(SEM["sIN"], t2)
                inc(ve.tensor_scalar(out=tmp[:], in0=lnw_rep[:], scalar1=rrev[:, 0:1],
                                     scalar2=None, op0=ALU.mult), "sVE")
                inc(ve.tensor_copy(bass.AP(fw, 0, [[512, 128], [64, 8], [1, 64]]),
                                   fwsm[:].rearrange("p a -> p a 1" if False else "p (a b) -> p a b", b=1).broadcast_to((128, 8, 64))), "sVE")
                ve.drain()
                inc(ve.tensor_scalar(out=fw[:], in0=fw[:], scalar1=128.0,
                                     scalar2=None, op0=ALU.mult), "sVE")
                inc(ve.memset(state[:], 0.0), "sVE")
                inc(ve.memset(m_run[:], NEG), "sVE")
            blk.vector(_v)
            t3 = CNT["sVE"]

            def _s3(se):
                se.wait_ge(SEM["sVE"], t3)
                inc(se.activation(out=fbar[:], in_=tmp[:], func=ACT.Exp), "sSE")
                inc(se.activation(out=fw[:], in_=fw[:], func=ACT.Exp), "sSE")
            blk.scalar(_s3)

            wnames = ["L0_attn_Wk", "L0_attn_Wv", "L0_attn_Wr", "L0_attn_Wo"]
            te_hist = []
            for wi in range(4):
                for ic in range(NCH):
                    slot = (wi * NCH + ic) % 2
                    def _yl(ye, wi=wi, ic=ic, slot=slot):
                        if len(te_hist) >= 2:
                            ye.wait_ge(SEM["sTE"], te_hist[-2])
                        dma(ye, out=t_nat[:, slot, :],
                                     in_=dd[wnames[wi]][ic * 128:(ic + 1) * 128, :], sem="sIN")
                    blk.sync(_yl)
                    here_in = CNT["sIN"]

                    def _tt(te, slot=slot, here_in=here_in):
                        te.wait_ge(SEM["sIN"], here_in)
                        te.wait_ge(SEM["sVE"], CNT["sVE"])
                        for jc in range(NCH):
                            ii = te.transpose(ps_tr[:, jc, :], t_nat[:, slot, jc * 128:(jc + 1) * 128], eye[:])
                        inc(ii, "sTE")
                    blk.tensor(_tt)
                    te_hist.append(CNT["sTE"])
                    here_te = CNT["sTE"]

                    def _ve(ve, slot=slot, here_te=here_te):
                        ve.wait_ge(SEM["sTE"], here_te)
                        ve.wait_ge(SEM["sOUT"], CNT["sOUT"])
                        inc(ve.tensor_copy(t_sh[:, slot, :].rearrange("p (a b) -> p a b", a=8),
                                           ps_tr[:]), "sVE")
                        ve.drain()
                        inc(ve.tensor_tensor(out=t_sl[:, slot, :].rearrange("p (a b) -> p a b", a=8),
                                             in0=ps_tr[:],
                                             in1=t_sh[:, slot, :].rearrange("p (a b) -> p a b", a=8),
                                             op=ALU.subtract), "sVE")
                    blk.vector(_ve)
                    here_ve = CNT["sVE"]

                    def _gs(ge, wi=wi, ic=ic, slot=slot, here_ve=here_ve):
                        ge.wait_ge(SEM["sVE"], here_ve)
                        dsth = bass.AP(s_wTh.tensor, wi * D * D + ic * 128,
                                       [[D, 128], [128 * D, 8], [1, 128]])
                        dma(ge, out=dsth,
                            in_=t_sh[:, slot, :].rearrange("p (a b) -> p a b", a=8), sem="sOUT")
                        dstl = bass.AP(s_wTl.tensor, wi * D * D + ic * 128,
                                       [[D, 128], [128 * D, 8], [1, 128]])
                        dma(ge, out=dstl,
                            in_=t_sl[:, slot, :].rearrange("p (a b) -> p a b", a=8), sem="sOUT")
                    blk.gpsimd(_gs)

            def _y3(ye):
                g = dd["L0_ffn_slot_g"]
                b = dd["L0_ffn_slot_b"]
                sc = dd["L0_ffn_scale"]
                dma(ye, out=grep[:], in_=bass.AP(g.tensor, 0, [[0, 128], [1, D]]), sem="sIN")
                dma(ye, out=brep[:], in_=bass.AP(b.tensor, 0, [[0, 128], [1, D]]), sem="sIN")
                dma(ye, out=screp[:], in_=bass.AP(sc.tensor, 0, [[0, 128], [1, 1]]), sem="sIN")
            blk.sync(_y3)
            t4 = CNT["sIN"]

            def _v4(ve):
                ve.wait_ge(SEM["sIN"], t4)
                inc(ve.tensor_scalar(out=grep[:], in0=grep[:], scalar1=screp[:, 0:1],
                                     scalar2=None, op0=ALU.mult), "sVE")
                inc(ve.tensor_scalar(out=brep[:], in0=brep[:], scalar1=screp[:, 0:1],
                                     scalar2=None, op0=ALU.mult), "sVE")
            blk.vector(_v4)

            for ec in range(NCH):
                slot = ec % 2
                def _yse(ye, ec=ec, slot=slot):
                    ye.wait_ge(SEM["sTE"], CNT["sTE"])
                    src = bass.AP(dd["L0_ffn_slot_embeds"].tensor, ec * 128 * D, [[D, 128], [1, D]])
                    dma(ye, out=t_nat[:, slot, :], in_=src, sem="sIN")
                blk.sync(_yse)
                in_se = CNT["sIN"]
                ln_stats(0, (lambda slot=slot: t_nat[:, slot, :]), D, [("sIN", in_se)])
                vready = CNT["sVE"]

                def _sse(ve, slot=slot, vready=vready):
                    ve.wait_ge(SEM["sVE"], vready)
                    ve.drain()
                    inc(ve.tensor_scalar(out=t_nat[:, slot, :], in0=t_nat[:, slot, :],
                                         scalar1=w_isg[:, 0:1], scalar2=w_nb[:, 0:1],
                                         op0=ALU.mult, op1=ALU.add), "sVE")
                blk.vector(_sse)
                sready = CNT["sVE"]

                def _vse(ve, slot=slot, sready=sready):
                    ve.wait_ge(SEM["sVE"], sready)
                    ve.drain()
                    inc(ve.tensor_tensor(out=t_nat[:, slot, :], in0=t_nat[:, slot, :],
                                         in1=grep[:], op=ALU.mult), "sVE")
                    ve.drain()
                    inc(ve.tensor_tensor(out=t_nat[:, slot, :], in0=t_nat[:, slot, :],
                                         in1=brep[:], op=ALU.add), "sVE")
                blk.vector(_vse)
                vready2 = CNT["sVE"]

                def _tse(te, slot=slot, vready2=vready2):
                    te.wait_ge(SEM["sVE"], vready2)
                    for jc in range(NCH):
                        ii = te.transpose(ps_tr[:, jc, :], t_nat[:, slot, jc * 128:(jc + 1) * 128], eye[:])
                    inc(ii, "sTE")
                blk.tensor(_tse)
                here_te = CNT["sTE"]

                def _vse2(ve, slot=slot, here_te=here_te):
                    ve.wait_ge(SEM["sTE"], here_te)
                    ve.wait_ge(SEM["sOUT"], CNT["sOUT"])
                    inc(ve.tensor_copy(t_sh[:, slot, :].rearrange("p (a b) -> p a b", a=8),
                                       ps_tr[:]), "sVE")
                    ve.drain()
                    inc(ve.tensor_tensor(out=t_sl[:, slot, :].rearrange("p (a b) -> p a b", a=8),
                                         in0=ps_tr[:],
                                         in1=t_sh[:, slot, :].rearrange("p (a b) -> p a b", a=8),
                                         op=ALU.subtract), "sVE")
                blk.vector(_vse2)
                here_ve = CNT["sVE"]

                def _gse(ge, ec=ec, slot=slot, here_ve=here_ve):
                    ge.wait_ge(SEM["sVE"], here_ve)
                    dsth = bass.AP(s_seTh.tensor, ec * 128, [[ES, 128], [128 * ES, 8], [1, 128]])
                    dma(ge, out=dsth,
                        in_=t_sh[:, slot, :].rearrange("p (a b) -> p a b", a=8), sem="sOUT")
                    dstl = bass.AP(s_seTl.tensor, ec * 128, [[ES, 128], [128 * ES, 8], [1, 128]])
                    dma(ge, out=dstl,
                        in_=t_sl[:, slot, :].rearrange("p (a b) -> p a b", a=8), sem="sOUT")
                blk.gpsimd(_gse)

            for src, dstb in [("L0_ffn_bg", bgT), ("L0_ffn_bu", buT)]:
                def _yb(ye, src=src):
                    ye.wait_ge(SEM["sTE"], CNT["sTE"])
                    dma(ye, out=bsm[:], in_=dd[src], sem="sIN")
                blk.sync(_yb)
                here_in = CNT["sIN"]

                def _tb(te, here_in=here_in):
                    te.wait_ge(SEM["sIN"], here_in)
                    te.wait_ge(SEM["sVE"], CNT["sVE"])
                    for fc in range(16):
                        ii = te.transpose(ps_b[:, fc * 16:(fc + 1) * 16], bsm[:, fc * 128:(fc + 1) * 128], eye[0:16, 0:16])
                    inc(ii, "sTE")
                blk.tensor(_tb)
                here_te = CNT["sTE"]

                def _vb(ve, dstb=dstb, here_te=here_te):
                    ve.wait_ge(SEM["sTE"], here_te)
                    inc(ve.tensor_copy(dstb[:].rearrange("p a b -> p (a b)"), ps_b[:, 0:256]), "sVE")
                blk.vector(_vb)

            barrier()

        m1 = contextlib.ExitStack()
        with m1:
            xb = sb(m1, "xb", [128, 3, 2, D])
            xn_tm = sb(m1, "xn_tm", [128, 2, 2, D])
            xnh = sb(m1, "xnh", [128, NCH, STOK], BF16)
            xnl = sb(m1, "xnl", [128, NCH, STOK], BF16)
            yfh = xnh
            yfl = xnl
            kp = sb(m1, "kp", [128, NCH, STOK])
            rp = sb(m1, "rp", [128, NCH, STOK])
            vt = sb(m1, "vt", [128, 2, D])
            kbar = sb(m1, "kbar", [128, D])
            rt = sb(m1, "rt", [128, NCH, 128])
            aa = sb(m1, "aa", [128, 2, 2, 128])
            wkTh = sb(m1, "wkTh", [128, NCH, D], BF16)
            wkTl = sb(m1, "wkTl", [128, NCH, D], BF16)
            wvTh = sb(m1, "wvTh", [128, NCH, D], BF16)
            wvTl = sb(m1, "wvTl", [128, NCH, D], BF16)
            wsth = sb(m1, "wsth", [128, NCH, D], BF16)
            wstl = sb(m1, "wstl", [128, NCH, D], BF16)
            ps_tr = psum(m1, "ps_tr", [128, 8, 128])
            ps_mm = psum(m1, "ps_mm", [128, 2, 512])
            ps_a = psum(m1, "ps_a", [128, 2, 512])
            ps_y = psum(m1, "ps_y", [128, 4, 128])
            ps_kv = psum(m1, "ps_kv", [128, 512])

            for wi, dsth_, dstl_ in [(0, wkTh, wkTl), (1, wvTh, wvTl)]:
                def _y(ye, wi=wi, dsth_=dsth_, dstl_=dstl_):
                    srch = bass.AP(s_wTh.tensor, wi * D * D, [[D, 128], [128 * D, 8], [1, D]])
                    dma(ye, out=dsth_[:], in_=srch, sem="sIN")
                    srcl = bass.AP(s_wTl.tensor, wi * D * D, [[D, 128], [128 * D, 8], [1, D]])
                    dma(ye, out=dstl_[:], in_=srcl, sem="sIN")
                blk.sync(_y)

            sp_tok = [None, None, None]
            slot_g = [[("sSE", 0), ("sVE", 0)] for _ in range(4)]
            tx_tok = [0, 0, 0, 0]

            def emit_front_m1(sti):
                t0 = sti * STOK
                pt = sti % 2
                p3 = sti % 3

                tok_x = [None]
                def _yx(ye, t0=t0, p3=p3):
                    if sp_tok[p3] is not None:
                        wtok(ye, sp_tok[p3])
                    srcx = bass.AP(dd["L0_attn_x"].tensor, t0 * D, [[D, 128], [128 * D, 2], [1, D]])
                    tok_x[0] = dma(ye, out=xb[:, p3, :, :], in_=srcx, sem="sIN")
                blk.sync(_yx)
                tok_x = tok_x[0]

                for j in range(2):
                    ti = 2 * sti + j
                    sl = pt * 2 + j
                    vr = ln_stats(sl, (lambda p3=p3, j=j: xb[:, p3, j, :]), D,
                                  [tok_x] + slot_g[sl])

                    def _sv(ve, pt=pt, p3=p3, j=j, sl=sl, vr=vr):
                        ve.wait_ge(SEM["sVE"], vr)
                        ve.wait_ge(SEM["sTE"], tx_tok[sl])
                        inc(ve.tensor_scalar(out=xn_tm[:, pt, j, :], in0=xb[:, p3, j, :],
                                             scalar1=w_isg[:, sl:sl + 1], scalar2=w_nb[:, sl:sl + 1],
                                             op0=ALU.mult, op1=ALU.add), "sVE")
                    blk.vector(_sv)
                    slot_g[sl] = [("sVE", CNT["sVE"])]

                    def _vk(ve, ti=ti, sl=sl, vr=vr):
                        ve.wait_ge(SEM["sVE"], vr)
                        inc(ve.tensor_copy(st_isg1[:, ti:ti + 1], w_isg[:, sl:sl + 1]), "sVE")
                        inc(ve.tensor_copy(st_nb1[:, ti:ti + 1], w_nb[:, sl:sl + 1]), "sVE")
                        inc(ve.tensor_copy(st_var1[:, ti:ti + 1], w_var[:, sl:sl + 1]), "sVE")
                    blk.vector(_vk)
                    slot_g[sl].append(("sVE", CNT["sVE"]))
                return CNT["sVE"]

            fr_m1 = {0: emit_front_m1(0)}
            m1_pstr_g = [("sVE", 0)]
            for sti in range(NST):
                t0 = sti * STOK
                pt = sti % 2
                p3 = sti % 3
                if sti + 1 < NST:
                    fr_m1[sti + 1] = emit_front_m1(sti + 1)
                sse_xn = fr_m1[sti]
                pstr_g = m1_pstr_g[0]

                tok_wr = [None]
                te_now = CNT["sTE"]
                def _ywr(ye, te_now=te_now):
                    ye.wait_ge(SEM["sTE"], te_now)
                    srch = bass.AP(s_wTh.tensor, 2 * D * D, [[D, 128], [128 * D, 8], [1, D]])
                    dma(ye, out=wsth[:], in_=srch, sem="sIN")
                    srcl = bass.AP(s_wTl.tensor, 2 * D * D, [[D, 128], [128 * D, 8], [1, D]])
                    tok_wr[0] = dma(ye, out=wstl[:], in_=srcl, sem="sIN")
                blk.sync(_ywr)
                tok_wr = tok_wr[0]

                for j in range(2):
                    def _tx(te, pt=pt, j=j, sse_xn=sse_xn, pstr_g=pstr_g):
                        te.wait_ge(SEM["sVE"], sse_xn)
                        wtok(te, pstr_g)
                        for c in range(NCH):
                            ii = te.transpose(ps_tr[:, c, :], xn_tm[:, pt, j, c * 128:(c + 1) * 128], eye[:])
                        inc(ii, "sTE")
                    blk.tensor(_tx)
                    te_t = CNT["sTE"]
                    tx_tok[pt * 2 + j] = te_t

                    def _vx(ve, j=j, te_t=te_t):
                        ve.wait_ge(SEM["sTE"], te_t)
                        dsth = bass.AP(xnh, j * 128, [[NCH * STOK, 128], [STOK, 8], [1, 128]])
                        inc(ve.tensor_copy(dsth, ps_tr[:]), "sVE")
                        ve.drain()
                        dstl = bass.AP(xnl, j * 128, [[NCH * STOK, 128], [STOK, 8], [1, 128]])
                        srch = bass.AP(xnh, j * 128, [[NCH * STOK, 128], [STOK, 8], [1, 128]])
                        inc(ve.tensor_tensor(out=dstl, in0=ps_tr[:], in1=srch,
                                             op=ALU.subtract), "sVE")
                    blk.vector(_vx)
                    pstr_g = ("sVE", CNT["sVE"])

                ve_xnfm = CNT["sVE"]

                sp_hist = [0, 0]
                for pi in range(2):
                    for oc in range(NCH):
                        slot = oc % 2
                        def _tp(te, pi=pi, oc=oc, slot=slot, ve_xnfm=ve_xnfm, tok_wr=tok_wr):
                            te.wait_ge(SEM["sVE"], ve_xnfm)
                            te.wait_ge(SEM["sSE"], sp_hist[-2])
                            if pi == 1:
                                wtok(te, tok_wr)
                            wh = wkTh if pi == 0 else wsth
                            wl = wkTl if pi == 0 else wstl
                            for c in range(NCH):
                                for qi, (l_, r_) in enumerate((
                                        (wh[:, c, oc * 128:(oc + 1) * 128], xnh[:, c, :]),
                                        (wh[:, c, oc * 128:(oc + 1) * 128], xnl[:, c, :]),
                                        (wl[:, c, oc * 128:(oc + 1) * 128], xnh[:, c, :]))):
                                    ii = te.matmul(ps_mm[:, slot, 0:STOK], l_, r_,
                                                   start=(c == 0 and qi == 0),
                                                   stop=(c == NCH - 1 and qi == 2))
                            inc(ii, "sTE")
                        blk.tensor(_tp)
                        te_p = CNT["sTE"]

                        def _sp(se, pi=pi, oc=oc, slot=slot, te_p=te_p):
                            se.wait_ge(SEM["sTE"], te_p)
                            if pi == 0:
                                inc(se.activation(out=kp[:, oc, :], in_=ps_mm[:, slot, 0:STOK],
                                                  func=ACT.Identity, scale=tf_fm[:, oc:oc + 1]), "sSE")
                            else:
                                inc(se.activation(out=rp[:, oc, :], in_=ps_mm[:, slot, 0:STOK],
                                                  func=ACT.Tanh, scale=0.5), "sSE")
                        blk.scalar(_sp)
                        sp_hist.append(CNT["sSE"])
                sse_r = CNT["sSE"]

                def _vr(ve, sse_r=sse_r):
                    ve.wait_ge(SEM["sSE"], sse_r)
                    inc(ve.tensor_scalar(out=rp[:].rearrange("p a b -> p (a b)"),
                                         in0=rp[:].rearrange("p a b -> p (a b)"),
                                         scalar1=0.5, scalar2=0.5,
                                         op0=ALU.mult, op1=ALU.add), "sVE")
                blk.vector(_vr)

                tok_wo = [None]
                te_now = CNT["sTE"]
                def _ywo(ye, te_now=te_now):
                    ye.wait_ge(SEM["sTE"], te_now)
                    srch = bass.AP(s_wTh.tensor, 3 * D * D, [[D, 128], [128 * D, 8], [1, D]])
                    dma(ye, out=wsth[:], in_=srch, sem="sIN")
                    srcl = bass.AP(s_wTl.tensor, 3 * D * D, [[D, 128], [128 * D, 8], [1, D]])
                    tok_wo[0] = dma(ye, out=wstl[:], in_=srcl, sem="sIN")
                blk.sync(_ywo)
                tok_wo = tok_wo[0]

                vv_hist = [CNT["sVE"], CNT["sVE"]]
                for j in range(2):
                    for ih in range(2):
                        slot = ih
                        def _tv(te, j=j, ih=ih, slot=slot):
                            te.wait_ge(SEM["sVE"], vv_hist[-2])
                            for c in range(NCH):
                                for qi, (l_, r_) in enumerate((
                                        (xnh[:, c, j * 128:(j + 1) * 128], wvTh[:, c, ih * 512:(ih + 1) * 512]),
                                        (xnh[:, c, j * 128:(j + 1) * 128], wvTl[:, c, ih * 512:(ih + 1) * 512]),
                                        (xnl[:, c, j * 128:(j + 1) * 128], wvTh[:, c, ih * 512:(ih + 1) * 512]))):
                                    ii = te.matmul(ps_mm[:, slot, :], l_, r_,
                                                   start=(c == 0 and qi == 0),
                                                   stop=(c == NCH - 1 and qi == 2))
                            inc(ii, "sTE")
                        blk.tensor(_tv)
                        te_v = CNT["sTE"]

                        def _vv(ve, j=j, ih=ih, slot=slot, te_v=te_v):
                            ve.wait_ge(SEM["sTE"], te_v)
                            inc(ve.tensor_copy(vt[:, j, ih * 512:(ih + 1) * 512],
                                               ps_mm[:, slot, :]), "sVE")
                        blk.vector(_vv)
                        vv_hist.append(CNT["sVE"])

                for ch in range(2):
                    def _tk(te, ch=ch, pstr_g=pstr_g, sse_r=sse_r):
                        te.wait_ge(SEM["sSE"], sse_r)
                        wtok(te, pstr_g)
                        for c in range(NCH):
                            ii = te.transpose(ps_tr[:, c, :], kp[:, c, ch * 128:(ch + 1) * 128], eye[:])
                        inc(ii, "sTE")
                    blk.tensor(_tk)
                    te_k = CNT["sTE"]

                    def _vk2(ve, ch=ch, te_k=te_k):
                        ve.wait_ge(SEM["sTE"], te_k)
                        inc(ve.tensor_tensor(out=kbar[:], in0=ps_tr[:].rearrange("p a b -> p (a b)"),
                                             in1=fbar[:], op=ALU.mult), "sVE")
                        inc(ve.tensor_tensor(out=rt[:], in0=rp[:, :, ch * 128:(ch + 1) * 128],
                                             in1=ftil[:], op=ALU.mult), "sVE")
                    blk.vector(_vk2)
                    pstr_g = ("sVE", CNT["sVE"])

                    for cp in range(4):
                        c2a = 2 * cp
                        sq = (cp % 2) * 256
                        def _ta(te, ch=ch, c2a=c2a, sq=sq):
                            te.wait_ge(SEM["sVE"], CNT["sVE"])
                            for qq in range(2):
                                c2 = c2a + qq
                                for par in range(2):
                                    ii = te.matmul(ps_a[:, par, sq + qq * 128:sq + qq * 128 + 128],
                                                   kp[par * 64:(par + 1) * 64, c2, ch * 128:(ch + 1) * 128],
                                                   rp[par * 64:(par + 1) * 64, c2, ch * 128:(ch + 1) * 128],
                                                   start=True, stop=True)
                                    inc(ii, "sTE")
                        blk.tensor(_ta)
                        te_a = CNT["sTE"]

                        def _va(ve, c2a=c2a, sq=sq, te_a=te_a):
                            ve.wait_ge(SEM["sTE"], te_a)
                            for par in range(2):
                                h = 2 * c2a + par
                                dhsl = bass.AP(dh, h * 128, [[H * 128, 128], [256, 2], [1, 128]])
                                inc(ve.tensor_tensor(out=aa[:, par, :, :],
                                                     in0=ps_a[:, par, sq:sq + 256].rearrange("p (a b) -> p a b", a=2),
                                                     in1=dhsl, op=ALU.mult), "sVE")
                        blk.vector(_va)
                        ve_a = CNT["sVE"]

                        def _ty(te, ch=ch, c2a=c2a, ve_a=ve_a):
                            te.wait_ge(SEM["sVE"], ve_a)
                            for qq in range(2):
                                c2 = c2a + qq
                                for par in range(2):
                                    h = 2 * c2 + par
                                    ii = te.matmul(ps_y[par * 64:(par + 1) * 64, qq, :],
                                                   vt[:, ch, h * 64:(h + 1) * 64],
                                                   aa[:, par, qq, :], start=True, stop=False,
                                                   skip_group_check=True)
                                    inc(ii, "sTE")
                                for par in range(2):
                                    h = 2 * c2 + par
                                    ii2 = te.matmul(ps_kv[par * 64:(par + 1) * 64, c2 * 64:(c2 + 1) * 64],
                                                    kbar[:, h * 64:(h + 1) * 64],
                                                    vt[:, ch, h * 64:(h + 1) * 64],
                                                    start=True, stop=True, skip_group_check=True)
                                    inc(ii2, "sTE")
                                for par in range(2):
                                    if par == 1:
                                        te.drain()
                                    ii = te.matmul(ps_y[par * 64:(par + 1) * 64, qq, :],
                                                   state[par * 64:(par + 1) * 64, c2 * 64:(c2 + 1) * 64],
                                                   rt[par * 64:(par + 1) * 64, c2, :],
                                                   start=False, stop=True, skip_group_check=True)
                                    inc(ii, "sTE")
                        blk.tensor(_ty)
                        te_y = CNT["sTE"]

                        def _vy(ve, ch=ch, c2a=c2a, te_y=te_y):
                            ve.wait_ge(SEM["sTE"], te_y)
                            dsth = bass.AP(xnh, c2a * STOK + ch * 128,
                                           [[NCH * STOK, 128], [STOK, 2], [1, 128]])
                            inc(ve.tensor_copy(dsth, ps_y[:, 0:2, :]), "sVE")
                            ve.drain()
                            dstl = bass.AP(xnl, c2a * STOK + ch * 128,
                                           [[NCH * STOK, 128], [STOK, 2], [1, 128]])
                            srch = bass.AP(xnh, c2a * STOK + ch * 128,
                                           [[NCH * STOK, 128], [STOK, 2], [1, 128]])
                            inc(ve.tensor_tensor(out=dstl, in0=ps_y[:, 0:2, :], in1=srch,
                                                 op=ALU.subtract), "sVE")
                        blk.vector(_vy)

                    te_kv = CNT["sTE"]

                    def _vs(ve, te_kv=te_kv):
                        ve.wait_ge(SEM["sTE"], te_kv)
                        inc(ve.tensor_tensor(out=state[:], in0=state[:], in1=fw[:], op=ALU.mult), "sVE")
                        ve.drain()
                        inc(ve.tensor_tensor(out=state[:], in0=state[:], in1=ps_kv[:], op=ALU.add), "sVE")
                    blk.vector(_vs)

                vo_hist = [CNT["sVE"], CNT["sVE"]]
                for ih in range(2):
                    for j in range(2):
                        slot = j
                        def _to(te, j=j, ih=ih, slot=slot, tok_wo=tok_wo):
                            wtok(te, tok_wo)
                            te.wait_ge(SEM["sVE"], vo_hist[-2])
                            for c in range(NCH):
                                for qi, (l_, r_) in enumerate((
                                        (yfh[:, c, j * 128:(j + 1) * 128], wsth[:, c, ih * 512:(ih + 1) * 512]),
                                        (yfh[:, c, j * 128:(j + 1) * 128], wstl[:, c, ih * 512:(ih + 1) * 512]),
                                        (yfl[:, c, j * 128:(j + 1) * 128], wsth[:, c, ih * 512:(ih + 1) * 512]))):
                                    ii = te.matmul(ps_mm[:, slot, :], l_, r_,
                                                   start=(c == 0 and qi == 0),
                                                   stop=(c == NCH - 1 and qi == 2))
                            inc(ii, "sTE")
                        blk.tensor(_to)
                        te_o = CNT["sTE"]

                        def _vo(ve, p3=p3, j=j, ih=ih, slot=slot, te_o=te_o):
                            ve.wait_ge(SEM["sTE"], te_o)
                            inc(ve.tensor_tensor(out=xb[:, p3, j, ih * 512:(ih + 1) * 512],
                                                 in0=xb[:, p3, j, ih * 512:(ih + 1) * 512],
                                                 in1=ps_mm[:, slot, :], op=ALU.add), "sVE")
                        blk.vector(_vo)
                        vo_hist.append(CNT["sVE"])

                ve_p = CNT["sVE"]

                def _gp(ge, t0=t0, p3=p3, ve_p=ve_p):
                    ge.wait_ge(SEM["sVE"], ve_p)
                    dst = bass.AP(s_p.tensor, t0 * D, [[D, 128], [128 * D, 2], [1, D]])
                    sp_tok[p3] = dma(ge, out=dst, in_=xb[:, p3, :, :], sem="sOUT")
                blk.gpsimd(_gp)
                m1_pstr_g[0] = pstr_g

            barrier()

        m2 = contextlib.ExitStack()
        with m2:
            xb = sb(m2, "xb2", [128, 3, 2, D])
            xf_tm = sb(m2, "xf_tm", [128, 2, 2, D])
            xfh = sb(m2, "xfh", [128, NCH, STOK], BF16)
            xfl = sb(m2, "xfl", [128, NCH, STOK], BF16)
            seTh = sb(m2, "seTh", [128, NCH, ES], BF16)
            seTl = sb(m2, "seTl", [128, NCH, ES], BF16)
            g2rep = sb(m2, "g2rep", [128, D])
            b2rep = sb(m2, "b2rep", [128, D])
            l_st = sb(m2, "l_st", [128, 2, ES])
            lT_st = sb(m2, "lT_st", [128, 2, NCH, 128])
            mx8 = sb(m2, "mx8", [128, NCH, 1])
            dumpe = sb(m2, "dumpe", [128, ES])
            ps_l = psum(m2, "ps_l", [128, 2, ES])
            ps_tr = psum(m2, "ps_tr", [128, 8, 128])

            def _y0(ye):
                srch = bass.AP(s_seTh.tensor, 0, [[ES, 128], [128 * ES, 8], [1, ES]])
                dma(ye, out=seTh[:], in_=srch, sem="sIN")
                srcl = bass.AP(s_seTl.tensor, 0, [[ES, 128], [128 * ES, 8], [1, ES]])
                dma(ye, out=seTl[:], in_=srcl, sem="sIN")
                g = dd["L0_ffn_norm_g"]
                b = dd["L0_ffn_norm_b"]
                dma(ye, out=g2rep[:], in_=bass.AP(g.tensor, 0, [[0, 128], [1, D]]), sem="sIN")
                dma(ye, out=b2rep[:], in_=bass.AP(b.tensor, 0, [[0, 128], [1, D]]), sem="sIN")
            blk.sync(_y0)

            xf_tok = [None, None, None]
            xfs_tok = [None, None]
            lst_tok = [None, None]
            lT_tok = [None, None]
            slot_g2 = [[("sSE", 0), ("sVE", 0)] for _ in range(4)]
            tx_tok2 = [0, 0, 0, 0]

            def emit_front_m2(sti):
                t0 = sti * STOK
                pt = sti % 2
                p3 = sti % 3

                tok_x = [None]
                def _yx(ye, t0=t0, p3=p3):
                    if xf_tok[p3] is not None:
                        wtok(ye, xf_tok[p3])
                    srcx = bass.AP(dd["L0_attn_x"].tensor, t0 * D, [[D, 128], [128 * D, 2], [1, D]])
                    tok_x[0] = dma(ye, out=xb[:, p3, :, :], in_=srcx, sem="sIN")
                blk.sync(_yx)
                tok_x = tok_x[0]

                for j in range(2):
                    ti = 2 * sti + j
                    sl = pt * 2 + j

                    def _s1(ve, pt=pt, j=j, ti=ti, sl=sl, tok_x=tok_x, g=tuple(slot_g2[sl])):
                        wtok(ve, tok_x)
                        for s_, v_ in g:
                            ve.wait_ge(SEM[s_], v_)
                        inc(ve.tensor_tensor(out=w_t[:, sl:sl + 1], in0=st_isg1[:, ti:ti + 1],
                                             in1=st_isg1[:, ti:ti + 1], op=ALU.mult), "sVE")
                        inc(ve.tensor_scalar(out=w_var[:, sl:sl + 1], in0=st_var1[:, ti:ti + 1],
                                             scalar1=-EPS, scalar2=None, op0=ALU.add), "sVE")
                        ve.drain()
                        inc(ve.tensor_scalar(out=w_var[:, sl:sl + 1], in0=w_var[:, sl:sl + 1],
                                             scalar1=w_t[:, sl:sl + 1], scalar2=EPS,
                                             op0=ALU.mult, op1=ALU.add), "sVE")
                    blk.vector(_s1)
                    vtgt2 = CNT["sVE"]

                    def _g2(ge, sl=sl, vtgt2=vtgt2):
                        ge.wait_ge(SEM["sVE"], vtgt2)
                        inc(ge.tensor_tensor(out=w_isg[:, sl:sl + 1], in0=w_var[:, sl:sl + 1],
                                             in1=cnh[:], op=ALU.pow), "sGP")
                    blk.gpsimd(_g2)
                    gtgt = CNT["sGP"]

                    def _s2(ve, pt=pt, p3=p3, j=j, ti=ti, sl=sl, gtgt=gtgt):
                        ve.wait_ge(SEM["sGP"], gtgt)
                        ve.wait_ge(SEM["sTE"], tx_tok2[sl])
                        if xfs_tok[pt] is not None:
                            wtok(ve, xfs_tok[pt])
                        inc(ve.tensor_scalar(out=w_t[:, sl:sl + 1], in0=st_isg1[:, ti:ti + 1],
                                             scalar1=w_isg[:, sl:sl + 1], scalar2=None,
                                             op0=ALU.mult), "sVE")
                        inc(ve.tensor_scalar(out=w_nb[:, sl:sl + 1], in0=st_nb1[:, ti:ti + 1],
                                             scalar1=w_isg[:, sl:sl + 1], scalar2=None,
                                             op0=ALU.mult), "sVE")
                        ve.drain()
                        inc(ve.tensor_scalar(out=xf_tm[:, pt, j, :], in0=xb[:, p3, j, :],
                                             scalar1=w_t[:, sl:sl + 1], scalar2=w_nb[:, sl:sl + 1],
                                             op0=ALU.mult, op1=ALU.add), "sVE")
                    blk.vector(_s2)
                    s2t = CNT["sVE"]
                    slot_g2[sl] = [("sVE", CNT["sVE"])]

                    def _v2(ve, pt=pt, j=j, s2t=s2t):
                        ve.wait_ge(SEM["sVE"], s2t)
                        ve.drain()
                        inc(ve.tensor_tensor(out=xf_tm[:, pt, j, :], in0=xf_tm[:, pt, j, :],
                                             in1=g2rep[:], op=ALU.mult), "sVE")
                        ve.drain()
                        inc(ve.tensor_tensor(out=xf_tm[:, pt, j, :], in0=xf_tm[:, pt, j, :],
                                             in1=b2rep[:], op=ALU.add), "sVE")
                    blk.vector(_v2)

                ve_xf = CNT["sVE"]

                def _gxf(ge, t0=t0, pt=pt, ve_xf=ve_xf):
                    ge.wait_ge(SEM["sVE"], ve_xf)
                    dst = bass.AP(s_xf.tensor, t0 * D, [[D, 128], [128 * D, 2], [1, D]])
                    xfs_tok[pt] = dma(ge, out=dst, in_=xf_tm[:, pt, :, :], sem="sOUT")
                blk.gpsimd(_gxf)
                xf_tok[p3] = ("sVE", CNT["sVE"])
                return ve_xf

            fr_m2 = {0: emit_front_m2(0)}
            pstr2 = ("sVE", 0)
            pl_g = [[], []]
            for sti in range(NST):
                t0 = sti * STOK
                pt = sti % 2
                if sti + 1 < NST:
                    fr_m2[sti + 1] = emit_front_m2(sti + 1)
                ve_xf = fr_m2[sti]

                vx_tok = [0, 0]
                for j in range(2):
                    def _tx(te, pt=pt, j=j, ve_xf=ve_xf, pstr2=pstr2):
                        te.wait_ge(SEM["sVE"], ve_xf)
                        wtok(te, pstr2)
                        for c in range(NCH):
                            ii = te.transpose(ps_tr[:, c, :], xf_tm[:, pt, j, c * 128:(c + 1) * 128], eye[:])
                        inc(ii, "sTE")
                    blk.tensor(_tx)
                    te_t = CNT["sTE"]
                    tx_tok2[pt * 2 + j] = te_t

                    def _vx(ve, j=j, te_t=te_t):
                        ve.wait_ge(SEM["sTE"], te_t)
                        dsth = bass.AP(xfh, j * 128, [[NCH * STOK, 128], [STOK, 8], [1, 128]])
                        inc(ve.tensor_copy(dsth, ps_tr[:]), "sVE")
                        ve.drain()
                        dstl = bass.AP(xfl, j * 128, [[NCH * STOK, 128], [STOK, 8], [1, 128]])
                        srch = bass.AP(xfh, j * 128, [[NCH * STOK, 128], [STOK, 8], [1, 128]])
                        inc(ve.tensor_tensor(out=dstl, in0=ps_tr[:], in1=srch,
                                             op=ALU.subtract), "sVE")
                    blk.vector(_vx)
                    pstr2 = ("sVE", CNT["sVE"])
                    vx_tok[j] = CNT["sVE"]

                for j in range(2):
                    ti = 2 * sti + j

                    def _tl(te, j=j, g=tuple(pl_g[j]), vx=vx_tok[j]):
                        te.wait_ge(SEM["sVE"], vx)
                        for s_, v_ in g:
                            te.wait_ge(SEM[s_], v_)
                        for hf in range(2):
                            for c in range(NCH):
                                for qi, (l_, r_) in enumerate((
                                        (xfh[:, c, j * 128:(j + 1) * 128], seTh[:, c, hf * 512:(hf + 1) * 512]),
                                        (xfh[:, c, j * 128:(j + 1) * 128], seTl[:, c, hf * 512:(hf + 1) * 512]),
                                        (xfl[:, c, j * 128:(j + 1) * 128], seTh[:, c, hf * 512:(hf + 1) * 512]))):
                                    ii = te.matmul(ps_l[:, j, hf * 512:(hf + 1) * 512], l_, r_,
                                                   start=(c == 0 and qi == 0),
                                                   stop=(c == NCH - 1 and qi == 2))
                            inc(ii, "sTE")
                    blk.tensor(_tl)
                    te_l = CNT["sTE"]

                    def _vl(ve, j=j, ti=ti, te_l=te_l):
                        ve.wait_ge(SEM["sTE"], te_l)
                        inc(ve.reduce_max(st_negmq[:, ti:ti + 1], ps_l[:, j, :],
                                          axis=AX.X, negate=True), "sVE")
                    blk.vector(_vl)
                    ve_mq = CNT["sVE"]
                    pl_g[j] = [("sVE", ve_mq)]

                    def _sl(se, j=j, ti=ti, ve_mq=ve_mq):
                        se.wait_ge(SEM["sVE"], ve_mq)
                        se.wait_ge(SEM["sSE"], CNT["sSE"])
                        inc(se.activation(out=dumpe[:], in_=ps_l[:, j, :], func=ACT.Exp,
                                          bias=st_negmq[:, ti:ti + 1],
                                          accum_out=st_zq[:, ti:ti + 1]), "sSE")
                    blk.scalar(_sl)
                    pl_g[j].append(("sSE", CNT["sSE"]))

                    def _sl2(ve, j=j, ve_mq=ve_mq):
                        ve.wait_ge(SEM["sVE"], ve_mq)
                        if lst_tok[j] is not None:
                            wtok(ve, lst_tok[j])
                        inc(ve.tensor_copy(l_st[:, j, :], ps_l[:, j, :]), "sVE")
                    blk.vector(_sl2)
                    se_l = ("sVE", CNT["sVE"])
                    pl_g[j].append(se_l)

                    def _gl(ge, j=j, ti=ti, se_l=se_l):
                        wtok(ge, se_l)
                        dst = bass.AP(s_lA.tensor, ti * 128 * ES, [[ES, 128], [1, ES]])
                        lst_tok[j] = dma(ge, out=dst, in_=l_st[:, j, :], sem="sOUT")
                    blk.gpsimd(_gl)

                    def _tlt(te, j=j, se_l=se_l, pstr2=pstr2):
                        wtok(te, se_l)
                        wtok(te, pstr2)
                        for c in range(NCH):
                            ii = te.transpose(ps_tr[:, c, :], l_st[:, j, c * 128:(c + 1) * 128], eye[:])
                        inc(ii, "sTE")
                    blk.tensor(_tlt)
                    te_lt = CNT["sTE"]

                    def _vlt(ve, j=j, te_lt=te_lt):
                        ve.wait_ge(SEM["sTE"], te_lt)
                        if lT_tok[j] is not None:
                            wtok(ve, lT_tok[j])
                        inc(ve.tensor_copy(lT_st[:, j, :, :], ps_tr[:]), "sVE")
                        ve.drain()
                        inc(ve.reduce_max(mx8[:], lT_st[:, j, :, :], axis=AX.X), "sVE")
                        ve.drain()
                        inc(ve.tensor_tensor(out=m_run[:], in0=m_run[:],
                                             in1=mx8[:].rearrange("p a b -> p (a b)"), op=ALU.max), "sVE")
                    blk.vector(_vlt)
                    ve_lt = CNT["sVE"]
                    pstr2 = ("sVE", ve_lt)

                    def _glt(ge, j=j, ti=ti, ve_lt=ve_lt):
                        ge.wait_ge(SEM["sVE"], ve_lt)
                        dst = bass.AP(s_lB.tensor, ti * 128, [[T, 128], [128 * T, 8], [1, 128]])
                        lT_tok[j] = dma(ge, out=dst, in_=lT_st[:, j, :, :], sem="sOUT")
                    blk.gpsimd(_glt)

            barrier()

        if max_phase < 3:
            return nc
        late = contextlib.ExitStack()
        with late:
            slots = sb(late, "slots", [128, NCH, ES])
            eoutT = sb(late, "eoutT", [128, NCH, D])

            f2 = contextlib.ExitStack()
            with f2:
                m_rep = sb(f2, "m_rep", [128, ES])
                zinv_rep = sb(f2, "zinv_rep", [128, ES])
                z_row = sb(f2, "z_row", [128, ES])
                zfin = sb(f2, "zfin", [128, 8])
                lhb = sb(f2, "lhb", [128, 2, 512])
                xfb = sb(f2, "xfb", [128, 2, D])
                eb = sb(f2, "eb", [128, 2, 512])
                ps_u = psum(f2, "ps_u", [128, 8, 512])

                def _gm(ge):
                    dma(ge, out=bass.AP(s_vec.tensor, ES, [[1, 128], [128, 8]]),
                                 in_=m_run[:], allow_slow_non_contiguous=True, sem="sOUT")
                blk.gpsimd(_gm)

                def _ym(ye):
                    ye.wait_ge(SEM["sOUT"], CNT["sOUT"])
                    dma(ye, out=m_rep[:],
                                 in_=bass.AP(s_vec.tensor, ES, [[0, 128], [1, ES]]), sem="sIN")
                blk.sync(_ym)

                def _vz(ve):
                    inc(ve.memset(z_row[:], 0.0), "sVE")
                blk.vector(_vz)

                te_hist2 = [0, 0]
                for hf in range(2):
                    for ti in range(NT):
                        bufi = ti % 2

                        def _yl(ye, hf=hf, ti=ti, bufi=bufi):
                            ye.wait_ge(SEM["sTE"], te_hist2[-2])
                            src = bass.AP(s_lA.tensor, ti * 128 * ES + hf * 512, [[ES, 128], [1, 512]])
                            dma(ye, out=lhb[:, bufi, :], in_=src, sem="sIN")
                            src2 = bass.AP(s_xf.tensor, ti * 128 * D, [[D, 128], [1, D]])
                            dma(ye, out=xfb[:, bufi, :], in_=src2, sem="sIN")
                        blk.sync(_yl)
                        in_l = CNT["sIN"]

                        def _ve(ve, hf=hf, bufi=bufi, in_l=in_l):
                            ve.wait_ge(SEM["sIN"], in_l)
                            ve.wait_ge(SEM["sTE"], te_hist2[-2])
                            inc(ve.tensor_tensor(out=eb[:, bufi, :], in0=lhb[:, bufi, :],
                                                 in1=m_rep[:, hf * 512:(hf + 1) * 512],
                                                 op=ALU.subtract), "sVE")
                        blk.vector(_ve)
                        ve_e = CNT["sVE"]

                        def _se(se, bufi=bufi, ve_e=ve_e):
                            se.wait_ge(SEM["sVE"], ve_e)
                            inc(se.activation(out=eb[:, bufi, :], in_=eb[:, bufi, :], func=ACT.Exp), "sSE")
                        blk.scalar(_se)
                        se_e = CNT["sSE"]

                        def _vz2(ve, hf=hf, bufi=bufi, se_e=se_e):
                            ve.wait_ge(SEM["sSE"], se_e)
                            inc(ve.tensor_tensor(out=z_row[:, hf * 512:(hf + 1) * 512],
                                                 in0=z_row[:, hf * 512:(hf + 1) * 512],
                                                 in1=eb[:, bufi, :], op=ALU.add), "sVE")
                        blk.vector(_vz2)

                        def _tu(te, ti=ti, bufi=bufi, se_e=se_e):
                            te.wait_ge(SEM["sSE"], se_e)
                            if ti == 0:
                                te.wait_ge(SEM["sVE"], CNT["sVE"])
                            for dc in range(NCH):
                                ii = te.matmul(ps_u[:, dc, :], xfb[:, bufi, dc * 128:(dc + 1) * 128],
                                               eb[:, bufi, :], start=(ti == 0), stop=(ti == NT - 1))
                            inc(ii, "sTE")
                        blk.tensor(_tu)
                        te_hist2.append(CNT["sTE"])

                    te_u = CNT["sTE"]

                    def _vu(ve, hf=hf, te_u=te_u):
                        ve.wait_ge(SEM["sTE"], te_u)
                        for dc in range(NCH):
                            ii = ve.tensor_copy(slots[:, dc, hf * 512:(hf + 1) * 512], ps_u[:, dc, :])
                        inc(ii, "sVE")
                    blk.vector(_vu)

                ve_slots = CNT["sVE"]

                def _tz(te, ve_slots=ve_slots):
                    te.wait_ge(SEM["sVE"], ve_slots)
                    for c in range(8):
                        ii = te.matmul(ps_u[:, 0, c:c + 1], z_row[:, c * 128:(c + 1) * 128],
                                       onescol[:], start=True, stop=True, skip_group_check=True)
                    inc(ii, "sTE")
                blk.tensor(_tz)
                te_z = CNT["sTE"]

                def _vzf(ve, te_z=te_z):
                    ve.wait_ge(SEM["sTE"], te_z)
                    inc(ve.tensor_copy(zfin[:], ps_u[:, 0, 0:8]), "sVE")
                    ve.drain()
                    inc(ve.reciprocal(zfin[:], zfin[:]), "sVE")
                blk.vector(_vzf)
                ve_zf = CNT["sVE"]

                def _gz(ge, ve_zf=ve_zf):
                    ge.wait_ge(SEM["sVE"], ve_zf)
                    dma(ge, out=bass.AP(s_vec.tensor, 2 * ES, [[1, 128], [128, 8]]),
                                 in_=zfin[:], allow_slow_non_contiguous=True, sem="sOUT")
                blk.gpsimd(_gz)

                def _yz(ye):
                    ye.wait_ge(SEM["sOUT"], CNT["sOUT"])
                    dma(ye, out=zinv_rep[:],
                                 in_=bass.AP(s_vec.tensor, 2 * ES, [[0, 128], [1, ES]]), sem="sIN")
                blk.sync(_yz)
                in_z = CNT["sIN"]

                def _vsl(ve, in_z=in_z):
                    ve.wait_ge(SEM["sIN"], in_z)
                    for dc in range(NCH):
                        inc(ve.tensor_tensor(out=slots[:, dc, :], in0=slots[:, dc, :],
                                             in1=zinv_rep[:], op=ALU.mult), "sVE")
                blk.vector(_vsl)

                if DEBUG_SPILL:
                    vnow_d = CNT["sVE"]
                    def _gd(ge, vnow_d=vnow_d):
                        ge.wait_ge(SEM["sVE"], vnow_d)
                        for dc in range(NCH):
                            dma(ge, bass.AP(s_dslots.tensor, dc * 128 * ES, [[ES, 128], [1, ES]]),
                                slots[:, dc, :], sem="sOUT")
                        dma(ge, bass.AP(s_dzm.tensor, 0, [[ES, 2], [1, ES]]),
                            m_rep[0:2, :], sem="sOUT")
                        dma(ge, bass.AP(s_dzm.tensor, 2 * ES, [[ES, 2], [1, ES]]),
                            zinv_rep[0:2, :], sem="sOUT")
                    blk.gpsimd(_gd)

                barrier()

            f25 = contextlib.ExitStack()
            with f25:
                wgs = sb(f25, "wgs", [128, 2, NCH, 256])
                wus = sb(f25, "wus", [128, 2, NCH, 256])
                wds = sb(f25, "wds", [128, 2, 4, D])
                gact = sb(f25, "gact", [128, 2, NE, SL])
                uact = sb(f25, "uact", [128, 2, NE, SL])
                bdrep = sb(f25, "bdrep", [128, D])
                ps_g = psum(f25, "ps_g", [128, 2, 512])
                ps_u2 = psum(f25, "ps_u2", [128, 2, 512])
                ps_e = psum(f25, "ps_e", [128, 2, 512])

                te_hist3 = [0, 0]
                qrot = ["sIN", "sCN"]
                grot = ["sGD", "sOUT"]
                ve_h_tok = {}
                se_g_hist = [0, 0]

                def emit_gate(e):
                    pe = e % 2
                    for fcp in range(8):
                        bufp = fcp % 2
                        tok_w = [None, None]

                        def _yw(ye, e=e, fcp=fcp, bufp=bufp, tok_w=tok_w):
                            ye.wait_ge(SEM["sTE"], te_hist3[-2])
                            srcw = bass.AP(dd["L0_ffn_Wg"].tensor, e * D * FF + fcp * 256,
                                           [[FF, 128], [128 * FF, 8], [1, 256]])
                            tok_w[0] = dma(ye, out=wgs[:, bufp, :, :], in_=srcw, sem=qrot[fcp % 2])
                            srcw2 = bass.AP(dd["L0_ffn_Wu"].tensor, e * D * FF + fcp * 256,
                                            [[FF, 128], [128 * FF, 8], [1, 256]])
                            tok_w[1] = dma(ye, out=wus[:, bufp, :, :], in_=srcw2, sem=qrot[(fcp + 1) % 2])
                        blk.sync(_yw)
                        tok_w = tuple(tok_w)

                        def _tg(te, e=e, fcp=fcp, bufp=bufp, tok_w=tok_w):
                            wtok(te, tok_w[0])
                            wtok(te, tok_w[1])
                            if fcp == 0:
                                te.wait_ge(SEM["sSE"], se_g_hist[-1])
                            for fcq in range(2):
                                fc = 2 * fcp + fcq
                                for dc in range(NCH):
                                    te.matmul(ps_g[:, fc // 8, (fc % 8) * 64:(fc % 8) * 64 + 64],
                                              wgs[:, bufp, dc, fcq * 128:(fcq + 1) * 128],
                                              slots[:, dc, e * 64:(e + 1) * 64],
                                              start=(dc == 0), stop=(dc == NCH - 1),
                                              skip_group_check=True)
                                    ii = te.matmul(ps_u2[:, fc // 8, (fc % 8) * 64:(fc % 8) * 64 + 64],
                                                   wus[:, bufp, dc, fcq * 128:(fcq + 1) * 128],
                                                   slots[:, dc, e * 64:(e + 1) * 64],
                                                   start=(dc == 0), stop=(dc == NCH - 1),
                                                   skip_group_check=True)
                            inc(ii, "sTE")
                        blk.tensor(_tg)
                        te_hist3.append(CNT["sTE"])
                    te_g = CNT["sTE"]

                    def _sg(se, e=e, pe=pe, te_g=te_g):
                        se.wait_ge(SEM["sTE"], te_g)
                        se.wait_ge(SEM["sVE"], CNT["sVE"])
                        for fc in range(16):
                            inc(se.activation(out=gact[:, pe, fc, :],
                                              in_=ps_g[:, fc // 8, (fc % 8) * 64:(fc % 8) * 64 + 64],
                                              func=ACT.Gelu, bias=bgT[:, fc, e:e + 1]), "sSE")
                        for fc in range(16):
                            inc(se.activation(out=uact[:, pe, fc, :],
                                              in_=ps_u2[:, fc // 8, (fc % 8) * 64:(fc % 8) * 64 + 64],
                                              func=ACT.Identity, bias=buT[:, fc, e:e + 1]), "sSE")
                    blk.scalar(_sg)
                    se_g_hist.append(CNT["sSE"])
                    se_g = CNT["sSE"]

                    def _vh(ve, pe=pe, se_g=se_g):
                        ve.wait_ge(SEM["sSE"], se_g)
                        inc(ve.tensor_tensor(out=gact[:, pe, :, :].rearrange("p a b -> p (a b)"),
                                             in0=gact[:, pe, :, :].rearrange("p a b -> p (a b)"),
                                             in1=uact[:, pe, :, :].rearrange("p a b -> p (a b)"),
                                             op=ALU.mult), "sVE")
                    blk.vector(_vh)
                    ve_h_tok[e] = CNT["sVE"]

                def emit_eout(e):
                    pe = e % 2
                    par = e % 2
                    ve_h = ve_h_tok[e]
                    for dh2 in range(2):
                        for fcp in range(4):
                            bufp = fcp % 2
                            tok_d = [None]

                            def _ywd(ge, e=e, dh2=dh2, fcp=fcp, bufp=bufp, tok_d=tok_d):
                                ge.wait_ge(SEM["sTE"], te_hist3[-2])
                                srcw = bass.AP(dd["L0_ffn_Wd"].tensor, (e * FF + fcp * 512) * D,
                                               [[D, 128], [128 * D, 4], [1, D]])
                                tok_d[0] = dma(ge, out=wds[:, bufp, :, :], in_=srcw, sem=grot[fcp % 2])
                            blk.gpsimd(_ywd)
                            tok_d = tok_d[0]

                            def _te2(te, e=e, pe=pe, par=par, dh2=dh2, fcp=fcp, bufp=bufp, tok_d=tok_d, ve_h=ve_h):
                                wtok(te, tok_d)
                                te.wait_ge(SEM["sVE"], ve_h)
                                if fcp == 0 and dh2 == 0 and par == 0:
                                    wtok(te, ps_e_g[0])
                                for fcq in range(4):
                                    fc = 4 * fcp + fcq
                                    ii = te.matmul(ps_e[par * 64:(par + 1) * 64, dh2, :],
                                                   gact[:, pe, fc, :],
                                                   wds[:, bufp, fcq, dh2 * 512:(dh2 + 1) * 512],
                                                   start=(fc == 0), stop=(fc == 15),
                                                   skip_group_check=True)
                                inc(ii, "sTE")
                            blk.tensor(_te2)
                            te_hist3.append(CNT["sTE"])

                    if par == 1:
                        te_e = CNT["sTE"]

                        def _yb(ye, e=e):
                            ye.wait_ge(SEM["sVE"], CNT["sVE"])
                            for q in range(2):
                                srcb = bass.AP(dd["L0_ffn_bd"].tensor, (e - 1 + q) * D, [[0, 64], [1, D]])
                                dma(ye, out=bdrep[q * 64:(q + 1) * 64, :], in_=srcb, sem="sIN")
                        blk.sync(_yb)
                        in_bd = ("sIN", CNT["sIN"])

                        def _vee(ve, e=e, te_e=te_e, in_bd=in_bd):
                            ve.wait_ge(SEM["sTE"], te_e)
                            wtok(ve, in_bd)
                            for dh2 in range(2):
                                inc(ve.tensor_tensor(out=eoutT[:, e // 2, dh2 * 512:(dh2 + 1) * 512],
                                                     in0=ps_e[:, dh2, :],
                                                     in1=bdrep[:, dh2 * 512:(dh2 + 1) * 512],
                                                     op=ALU.add), "sVE")
                        blk.vector(_vee)
                        ps_e_g[0] = ("sVE", CNT["sVE"])

                ps_e_g = [("sVE", 0)]
                emit_gate(0)
                for e in range(1, NE):
                    emit_gate(e)
                    emit_eout(e - 1)
                emit_eout(NE - 1)

                barrier()

            f3 = contextlib.ExitStack()
            with f3:
                lnz = sb(f3, "lnz", [128, NT])
                bq = sb(f3, "bq", [128, NT])
                brep3 = sb(f3, "brep3", [128, 2, 512])
                comb = sb(f3, "comb", [128, 2, NCH, 512])
                pbuf = sb(f3, "pbuf", [128, 2, 4, D])
                obuf = sb(f3, "obuf", [128, 2, 4, D])
                ps_f = psum(f3, "ps_f", [128, 4, 512])

                def _sB(se):
                    inc(se.activation(out=lnz[:], in_=st_zq[:], func=ACT.Ln), "sSE")
                blk.scalar(_sB)
                se_b = CNT["sSE"]

                def _vB(ve, se_b=se_b):
                    ve.wait_ge(SEM["sSE"], se_b)
                    inc(ve.tensor_tensor(out=bq[:], in0=lnz[:], in1=st_negmq[:],
                                         op=ALU.subtract), "sVE")
                blk.vector(_vB)
                ve_b = CNT["sVE"]

                def _gB(ge, ve_b=ve_b):
                    ge.wait_ge(SEM["sVE"], ve_b)
                    dma(ge, out=bass.AP(s_B.tensor, 0, [[1, 128], [128, NT]]),
                        in_=bq[:], allow_slow_non_contiguous=True, sem="sOUT")
                blk.gpsimd(_gB)
                out_b = ("sOUT", CNT["sOUT"])

                qrot3 = ["sIN", "sCN"]
                ob_tok = [None, None]
                ve_hist = [0, 0]
                for q in range(T // 512):
                    n0 = q * 512
                    pq = q % 2
                    tok_q = [None, None, None]

                    def _yq(ye, n0=n0, pq=pq, q=q, out_b=out_b, tok_q=tok_q):
                        wtok(ye, out_b)
                        ye.wait_ge(SEM["sVE"], ve_hist[-2])
                        tok_q[0] = dma(ye, out=brep3[:, pq, :],
                                       in_=bass.AP(s_B.tensor, n0, [[0, 128], [1, 512]]), sem=qrot3[q % 2])
                        srcl = bass.AP(s_lB.tensor, n0, [[T, 128], [128 * T, 8], [1, 512]])
                        tok_q[1] = dma(ye, out=comb[:, pq, :, :], in_=srcl, sem=qrot3[(q + 1) % 2])
                        srcp = bass.AP(s_p.tensor, n0 * D, [[D, 128], [128 * D, 4], [1, D]])
                        tok_q[2] = dma(ye, out=pbuf[:, pq, :, :], in_=srcp, sem=qrot3[q % 2])
                    blk.sync(_yq)
                    tok_q = tuple(tok_q)

                    def _vq(ve, pq=pq, tok_q=tok_q):
                        for t_ in tok_q[:2]:
                            wtok(ve, t_)
                        if ob_tok[pq] is not None:
                            wtok(ve, ob_tok[pq])
                        b3 = bass.AP(brep3, pq * 512, [[1024, 128], [0, 8], [1, 512]])
                        inc(ve.tensor_tensor(out=comb[:, pq, :, :], in0=comb[:, pq, :, :],
                                             in1=b3, op=ALU.subtract), "sVE")
                    blk.vector(_vq)
                    ve_q = CNT["sVE"]

                    def _sq(se, pq=pq, ve_q=ve_q):
                        se.wait_ge(SEM["sVE"], ve_q)
                        inc(se.activation(out=comb[:, pq, :, :].rearrange("p a b -> p (a b)"),
                                          in_=comb[:, pq, :, :].rearrange("p a b -> p (a b)"),
                                          func=ACT.Exp), "sSE")
                    blk.scalar(_sq)
                    se_q = CNT["sSE"]

                    for ns in range(4):
                        for dh2 in range(2):
                            slot = (ns * 2 + dh2) % 4

                            def _tf(te, pq=pq, ns=ns, dh2=dh2, slot=slot, se_q=se_q):
                                te.wait_ge(SEM["sSE"], se_q)
                                te.wait_ge(SEM["sVE"], CNT["sVE"])
                                for c in range(NCH):
                                    ii = te.matmul(ps_f[:, slot, :],
                                                   comb[:, pq, c, ns * 128:(ns + 1) * 128],
                                                   eoutT[:, c, dh2 * 512:(dh2 + 1) * 512],
                                                   start=(c == 0), stop=(c == NCH - 1))
                                inc(ii, "sTE")
                            blk.tensor(_tf)
                            te_f = CNT["sTE"]

                            def _vf(ve, pq=pq, ns=ns, dh2=dh2, slot=slot, te_f=te_f, tok_q=tok_q):
                                ve.wait_ge(SEM["sTE"], te_f)
                                wtok(ve, tok_q[2])
                                inc(ve.tensor_tensor(out=obuf[:, pq, ns, dh2 * 512:(dh2 + 1) * 512],
                                                     in0=pbuf[:, pq, ns, dh2 * 512:(dh2 + 1) * 512],
                                                     in1=ps_f[:, slot, :], op=ALU.add), "sVE")
                            blk.vector(_vf)

                    ve_o = CNT["sVE"]
                    ve_hist.append(ve_o)

                    def _go(ge, n0=n0, pq=pq, ve_o=ve_o):
                        ge.wait_ge(SEM["sVE"], ve_o)
                        dst = bass.AP(d_out.tensor, n0 * D, [[D, 128], [128 * D, 4], [1, D]])
                        ob_tok[pq] = dma(ge, out=dst, in_=obuf[:, pq, :, :], sem="sOUT")
                    blk.gpsimd(_go)

                def _gfin(ge):
                    ge.wait_ge(SEM["sOUT"], CNT["sOUT"])
                    ge.wait_ge(SEM["sGD"], CNT["sGD"])
                blk.gpsimd(_gfin)

    return nc


def _host_consts():
    eye = np.eye(128, dtype=np.float32)
    s_idx = np.arange(128, dtype=np.float32)[:, None]
    t_idx = np.arange(128, dtype=np.float32)[None, :]
    delta = np.where(t_idx >= s_idx, t_idx - s_idx, np.float32(1e9)).astype(np.float32)
    ramp1 = np.broadcast_to(t_idx + 1.0, (128, 128)).astype(np.float32).copy()
    rrev = (127.0 - np.arange(128, dtype=np.float32))[:, None].copy()
    ones = np.ones((128, 1), dtype=np.float32)
    return {"c_eye": eye, "c_delta": delta, "c_ramp1": ramp1,
            "c_rrev": rrev, "c_ones": ones}


def kernel(**inputs):
    x = np.ascontiguousarray(np.asarray(inputs["L0_attn_x"], dtype=np.float32))
    B, T = x.shape[0], x.shape[1]
    nc = build_core_program(T=T)
    consts = _host_consts()
    base = {k: np.ascontiguousarray(np.asarray(v, dtype=np.float32))
            for k, v in inputs.items() if k != "L0_attn_x"}
    in_maps = []
    for b in range(B):
        m = dict(base)
        m.update(consts)
        m["L0_attn_x"] = x[b]
        in_maps.append(m)
    res = run_bass_kernel_spmd(nc, in_maps, core_ids=list(range(B)))
    return np.stack([np.asarray(res.results[b]["out"], dtype=np.float32) for b in range(B)], axis=0)
